# version 01
# baseline (speedup 1.0000x reference)
import math
from contextlib import ExitStack

import numpy as np

import concourse.bass as bass
import concourse.mybir as mybir
from concourse.bass_utils import run_bass_kernel_spmd

F32 = mybir.dt.float32
BF16 = mybir.dt.bfloat16
U8 = mybir.dt.uint8
I32 = mybir.dt.int32
AF = mybir.ActivationFunctionType
ALU = mybir.AluOpType
AX = mybir.AxisListType.X

B, S, E = 4, 2048, 4096
HA, DH, KVH = 32, 128, 8
NB, TPB, OWN = 8, 128, 1024
NCH = E // 128
KREM = 410
SCALE = 1.0 / math.sqrt(128.0)
NEGM = -1.0e9
SIMM = -2000.0
LNEPS = 1e-5
TWO_PI = 2.0 * math.pi
N_JUNCTIONS = 4
RG = [[0, 1], [2, 3], [4, 5], [6, 7]]


class Cs:

    def __init__(self, ctx, nc, name):
        self.h = ctx.enter_context(nc.semaphore(name))
        self.n = 0

    def inc(self, ins, k=1):
        ins.then_inc(self.h, k)
        self.n += k
        return self.n


def build_nc(n_junctions=N_JUNCTIONS):
    nc = bass.Bass("TRN2", target_bir_lowering=False)

    def din(name, shape, dtype=F32):
        return nc.dram_tensor(name, shape, dtype, kind="ExternalInput").ap()

    x_in = din("x", [NB, TPB, E])
    qkvw = din("qkvw", [E, 3 * E])
    qkvb = din("qkvb", [1, 3 * E])
    outw = din("outw", [E, E])
    outb = din("outb", [1, E])
    wqT = din("wqT", [E, E])
    wkT = din("wkT", [E, KVH * DH])
    wvT = din("wvT", [E, KVH * DH])
    woT = din("woT", [E, E])
    hp1 = din("hp1", [HA, DH, DH])
    hp2 = din("hp2", [HA, DH, DH])
    hb1 = din("hb1", [DH, HA])
    hb2 = din("hb2", [DH, HA])
    wbT = din("wbT", [E, E])
    bbase = din("bbase", [1, E])
    wrT = din("wrT", [E, 8])
    laT = din("laT", [E, 128])
    lbT = din("lbT", [128, E])
    posv = din("posv", [TPB, NB])
    dmA = din("dmA", [TPB, 512])
    dmS = din("dmS", [TPB, 512])
    id128 = din("id128", [128, 128])

    y_out = nc.dram_tensor("y", [NB, TPB, E], F32, kind="ExternalOutput").ap()

    xres = nc.dram_tensor("xres", [NB, TPB, E], F32).ap()
    qrtA = nc.dram_tensor("qrtA", [E, OWN], BF16).ap()
    qrtB = nc.dram_tensor("qrtB", [E, OWN], BF16).ap()
    krtA_in = nc.dram_tensor("krtA_in", [E, OWN], BF16).ap()
    vA_in = nc.dram_tensor("vA_in", [OWN, E], BF16).ap()
    krtA_out = nc.dram_tensor("krtA_out", [2 * E, OWN], BF16).ap()
    vA_out = nc.dram_tensor("vA_out", [2 * OWN, E], BF16).ap()
    krtB_in = nc.dram_tensor("krtB_in", [KVH * DH, OWN], BF16).ap()
    vB_in = nc.dram_tensor("vB_in", [OWN, KVH * DH], BF16).ap()
    krtB_out = nc.dram_tensor("krtB_out", [2 * KVH * DH, OWN], BF16).ap()
    vB_out = nc.dram_tensor("vB_out", [2 * OWN, KVH * DH], BF16).ap()
    ctx_hbm = nc.dram_tensor("ctx_hbm", [OWN, E], BF16).ap()

    with ExitStack() as ctx:
        en = ctx.enter_context
        _uid = [0]

        def sbuf(name, shape, dtype, **kw):
            _uid[0] += 1
            return nc.sbuf_tensor(f"{name}_{_uid[0]}", shape, dtype, **kw)


        ones1 = en(sbuf("ones1", [1, 128], BF16))
        id_s = en(sbuf("id_s", [128, 128], BF16))
        hb1_s = en(sbuf("hb1_s", [DH, HA], F32))
        hb2_s = en(sbuf("hb2_s", [DH, HA], F32))
        cosf = en(sbuf("cosf", [TPB, NB, DH], F32))
        sinf = en(sbuf("sinf", [TPB, NB, DH], F32))
        dmA_s = en(sbuf("dmA_s", [TPB, 512], BF16))
        dmS_s = en(sbuf("dmS_s", [TPB, 512], BF16))
        negt = en(sbuf("negt", [128, 2048], F32))
        zerot = en(sbuf("zerot", [128, 2048], BF16))

        ps_big = en(nc.psum_tensor("ps_big", [128, 2048], F32))
        psA = en(nc.psum_tensor("psA", [128, 512], F32))
        psB = en(nc.psum_tensor("psB", [128, 512], F32))
        psC = en(nc.psum_tensor("psC", [128, 512], F32))
        psD = en(nc.psum_tensor("psD", [128, 512], F32))
        PS = [psA, psB]

        ALL_DMA = []

        def dsem(name):
            c = Cs(ctx, nc, name)
            ALL_DMA.append(c)
            return c

        ph = Cs(ctx, nc, "ph")
        ldw = dsem("ldw")
        lds = dsem("lds")
        tm = Cs(ctx, nc, "tm")
        vm = Cs(ctx, nc, "vm")
        sm = Cs(ctx, nc, "sm")
        gm = Cs(ctx, nc, "gm")
        ccs = Cs(ctx, nc, "ccs")
        wts = [dsem(f"wts{i}") for i in range(2)]
        nxs = [dsem(f"nxs{i}") for i in range(2)]
        ntr = [dsem(f"ntr{i}") for i in range(2)]
        rts = [dsem(f"rts{i}") for i in range(2)]
        sts = [dsem(f"sts{i}") for i in range(2)]
        svs = [dsem(f"svs{i}") for i in range(2)]
        lxs = [dsem(f"lxs{i}") for i in range(2)]
        sxps = [dsem(f"sxps{i}") for i in range(2)]
        kvs = [dsem(f"kvs{i}") for i in range(2)]
        qs = [dsem(f"qs{i}") for i in range(2)]
        ptrs = dsem("ptrs")
        ces = [dsem(f"ces{i}") for i in range(2)]
        cls = [dsem(f"cls{i}") for i in range(2)]
        cts = [dsem(f"cts{i}") for i in range(2)]
        htrs = dsem("htrs")
        fws = dsem("fws")
        p12s = dsem("p12s")

        block = en(nc.Block())
        T = block.tensor
        V = block.vector
        SC = block.scalar
        SY = block.sync
        G = block.gpsimd

        def barrier():
            ph.n += 5
            tgt = ph.n
            tot = [(c, c.n) for c in ALL_DMA if c.n]
            for dec in (T, V, SC, SY, G):
                def f(eng, _t=tgt, _tot=tot):
                    for c, n in _tot:
                        eng.wait_ge(c.h, n)
                    eng.drain().then_inc(ph.h, 1)
                    eng.wait_ge(ph.h, _t)
                dec(f)

        def sap(t, offset, dims):
            return bass.AP(t, offset, [[t.ap().ap[0][0], 128]] + dims)

        def f(g):
            g.memset(ones1[:, :], 1.0)
            g.memset(negt[:, :], NEGM)
            gm.inc(g.memset(zerot[:, :], 0.0))
            for dst, src in [
                (id_s[:, :], id128[:, :]),
                (dmA_s[:, :], dmA[:, :]),
                (dmS_s[:, :], dmS[:, :]),
            ]:
                ldw.inc(g.dma_start(out=dst, in_=src), 16)
        G(f)

        def f(s):
            for dst, src in [
                (hb1_s[:, :], hb1[:, :]),
                (hb2_s[:, :], hb2[:, :]),
            ]:
                lds.inc(s.dma_start(out=dst, in_=src), 16)
        SY(f)

        setup_stack = ExitStack()
        se_ = setup_stack.enter_context
        iota_s = se_(sbuf("iota_s", [TPB, 64], F32))
        freq_s = se_(sbuf("freq_s", [TPB, 64], F32))
        pos_s = se_(sbuf("pos_s", [TPB, NB], F32))
        angt = se_(sbuf("angt", [TPB, 64], F32))
        rr1 = se_(sbuf("rr1", [TPB, 64], F32))
        rri = se_(sbuf("rri", [TPB, 64], I32))
        rrn = se_(sbuf("rrn", [TPB, 64], F32))
        trig = se_(sbuf("trig", [TPB, 64], F32))

        def f(g):
            gm.inc(g.iota(iota_s[:, :], [[1, 64]], channel_multiplier=0,
                          allow_small_or_imprecise_dtypes=True))
        G(f)
        gmt = gm.n

        def f(s):
            s.wait_ge(gm.h, gmt)
            sm.inc(s.activation(out=freq_s[:, :], in_=iota_s[:, :], func=AF.Exp,
                                scale=float(-math.log(10000.0) / 64.0)))
        SC(f)
        freq_t = sm.n

        def f(s):
            lds.inc(s.dma_start(out=pos_s[:, :], in_=posv[:, :]), 16)
        SY(f)
        pos_t = lds.n

        for tb in range(NB):
            for shift, dst, neg_first in ((math.pi / 2, cosf, False), (0.0, sinf, True)):
                def f(v, tb=tb, shift=shift):
                    v.wait_ge(sm.h, freq_t)
                    v.wait_ge(lds.h, pos_t)
                    v.tensor_scalar(out=angt[:, :], in0=freq_s[:, :], scalar1=pos_s[:, tb:tb + 1],
                                    scalar2=float(shift), op0=ALU.mult, op1=ALU.add)
                    v.drain()
                    v.tensor_scalar(out=rr1[:, :], in0=angt[:, :], scalar1=float(1.0 / TWO_PI),
                                    scalar2=None, op0=ALU.mult)
                    v.drain()
                    v.tensor_copy(out=rri[:, :], in_=rr1[:, :])
                    v.drain()
                    v.tensor_copy(out=rrn[:, :], in_=rri[:, :])
                    v.drain()
                    v.scalar_tensor_tensor(out=angt[:, :], in0=rrn[:, :], scalar=float(-TWO_PI),
                                           in1=angt[:, :], op0=ALU.mult, op1=ALU.add)
                    v.drain()
                    v.tensor_scalar(out=rr1[:, :], in0=angt[:, :], scalar1=float(math.pi),
                                    scalar2=float(-TWO_PI), op0=ALU.is_gt, op1=ALU.mult)
                    v.drain()
                    v.tensor_add(angt[:, :], angt[:, :], rr1[:, :])
                    v.drain()
                    v.tensor_scalar(out=rr1[:, :], in0=angt[:, :], scalar1=float(-math.pi),
                                    scalar2=float(TWO_PI), op0=ALU.is_lt, op1=ALU.mult)
                    v.drain()
                    vm.inc(v.tensor_add(angt[:, :], angt[:, :], rr1[:, :]))
                V(f)
                ang_t = vm.n

                def f(s, ang_t=ang_t):
                    s.wait_ge(vm.h, ang_t)
                    sm.inc(s.activation(out=trig[:, :], in_=angt[:, :], func=AF.Sin))
                SC(f)
                sin_t = sm.n

                def f(v, tb=tb, dst=dst, neg_first=neg_first, sin_t=sin_t):
                    v.wait_ge(sm.h, sin_t)
                    if neg_first:
                        v.tensor_scalar(out=dst[:, tb, 0:64], in0=trig[:, :], scalar1=-1.0,
                                        scalar2=None, op0=ALU.mult)
                    else:
                        v.tensor_copy(out=dst[:, tb, 0:64], in_=trig[:, :])
                    vm.inc(v.tensor_copy(out=dst[:, tb, 64:128], in_=trig[:, :]))
                V(f)

        barrier()
        setup_stack.close()

        def normalize_phase(src_dram, xnT):
            with ExitStack() as pctx:
                pe = pctx.enter_context
                xt = [pe(sbuf(f"nx{i}", [128, E], F32)) for i in range(2)]
                xn = [pe(sbuf(f"nn{i}", [128, E], BF16)) for i in range(2)]
                sq = pe(sbuf("nsq", [128, E], BF16))
                col = {k: pe(sbuf("nc_" + k, [128, 1], F32))
                       for k in ["s1", "s2", "mean", "ms", "var", "rec", "y0", "y2", "tt", "t2", "inv"]}
                tr_done = {}
                xn_done = {}
                for tb in range(NB):
                    bf = tb % 2

                    def f(s, tb=tb, bf=bf):
                        if tb >= 2:
                            s.wait_ge(vm.h, xn_done[tb - 2])
                        nxs[bf].inc(s.dma_start(out=xt[bf][:, :], in_=src_dram[tb, :, :]), 16)
                    SY(f)
                    x_t = nxs[bf].n

                    def f(v, tb=tb, bf=bf, x_t=x_t):
                        v.wait_ge(nxs[bf].h, x_t)
                        if tb >= 2:
                            v.wait_ge(ntr[bf].h, tr_done[tb - 2])
                        v.reduce_sum(col["s1"][:, :], xt[bf][:, :], AX)
                        v.scalar_tensor_tensor(out=sq[:, :], in0=xt[bf][:, :], scalar=1.0,
                                               in1=xt[bf][:, :], op0=ALU.mult, op1=ALU.mult,
                                               accum_out=col["s2"][:, :])
                        v.drain()
                        v.tensor_scalar(out=col["mean"][:, :], in0=col["s1"][:, :],
                                        scalar1=float(1.0 / E), scalar2=None, op0=ALU.mult)
                        v.drain()
                        v.tensor_mul(col["ms"][:, :], col["mean"][:, :], col["mean"][:, :])
                        v.drain()
                        v.tensor_scalar(out=col["var"][:, :], in0=col["s2"][:, :],
                                        scalar1=float(1.0 / E), scalar2=col["ms"][:, :],
                                        op0=ALU.mult, op1=ALU.subtract)
                        v.drain()
                        v.tensor_scalar(out=col["var"][:, :], in0=col["var"][:, :],
                                        scalar1=float(LNEPS), scalar2=None, op0=ALU.add)
                        v.drain()
                        vm.inc(v.reciprocal(col["rec"][:, :], col["var"][:, :]))
                    V(f)
                    rec_t = vm.n

                    def f(s, rec_t=rec_t):
                        s.wait_ge(vm.h, rec_t)
                        sm.inc(s.activation(out=col["y0"][:, :], in_=col["rec"][:, :], func=AF.Sqrt))
                    SC(f)
                    y0_t = sm.n

                    def f(v, tb=tb, bf=bf, y0_t=y0_t):
                        v.wait_ge(sm.h, y0_t)
                        v.tensor_mul(col["y2"][:, :], col["y0"][:, :], col["y0"][:, :])
                        v.drain()
                        v.tensor_mul(col["tt"][:, :], col["var"][:, :], col["y2"][:, :])
                        v.drain()
                        v.tensor_scalar(out=col["t2"][:, :], in0=col["tt"][:, :], scalar1=-0.5,
                                        scalar2=1.5, op0=ALU.mult, op1=ALU.add)
                        v.drain()
                        v.tensor_mul(col["inv"][:, :], col["y0"][:, :], col["t2"][:, :])
                        v.drain()
                        vm.inc(v.tensor_scalar(out=xn[bf][:, :], in0=xt[bf][:, :],
                                               scalar1=col["mean"][:, :], scalar2=col["inv"][:, :],
                                               op0=ALU.subtract, op1=ALU.mult))
                    V(f)
                    xn_done[tb] = vm.n
                    xn_t = vm.n

                    def f(s, tb=tb, bf=bf, xn_t=xn_t):
                        s.wait_ge(vm.h, xn_t)
                        ntr[bf].inc(s.dma_start_transpose(
                            out=xnT[:, :, tb * 128:(tb + 1) * 128], in_=xn[bf][:, :]), 16)
                    SC(f)
                    tr_done[tb] = ntr[bf].n

                barrier()

        class ProjCtx:

            def __init__(self, pe, wname):
                self.wt = [pe(sbuf(f"{wname}{i}", [128, NCH, 512], BF16)) for i in range(2)]
                self.brow = [pe(sbuf(f"{wname}b{i}", [1, 512], BF16)) for i in range(2)]
                self.mmdone = {}
                self.psum_free = {0: 0, 1: 0}
                self.ogc = 0

        def og_loop(pc, stat, w_src, n_og, epi, bias_src=None, lora=False, hwT=None,
                    lb_s=None):
            if True:
                wld = {}
                for og in range(n_og):
                    go = pc.ogc
                    pc.ogc += 1
                    wb = go % 2

                    def f(g, og=og, wb=wb, go=go):
                        if go >= 2:
                            g.wait_ge(tm.h, pc.mmdone[go - 2])
                        wts[wb].inc(g.dma_start(out=pc.wt[wb][:, :, :], in_=w_src(og)), 16)
                        if bias_src is not None:
                            wts[wb].inc(g.dma_start(out=pc.brow[wb][:, :], in_=bias_src(og)), 16)
                    G(f)
                    wld[og] = wts[wb].n

                    for tb in range(NB):
                        pb = (go * NB + tb) % 2
                        psum = PS[pb]

                        def f(t, og=og, tb=tb, wb=wb, pb=pb, psum=psum):
                            t.wait_ge(wts[wb].h, wld[og])
                            if pc.psum_free[pb]:
                                t.wait_ge(vm.h, pc.psum_free[pb])
                            nextra = (1 if lora else 0) + (1 if bias_src is not None else 0)
                            last = None
                            for c in range(NCH):
                                last = t.matmul(psum[:, :], stat[:, c, tb * 128:(tb + 1) * 128],
                                                pc.wt[wb][:, c, :], start=(c == 0),
                                                stop=(c == NCH - 1 and nextra == 0))
                            if lora:
                                last = t.matmul(psum[:, :], hwT[:, tb, :],
                                                lb_s[:, og * 512:(og + 1) * 512],
                                                start=False, stop=(bias_src is None))
                            if bias_src is not None:
                                last = t.matmul(psum[:, :], ones1[:, :], pc.brow[wb][:, :],
                                                start=False, stop=True)
                            tm.inc(last)
                        T(f)
                        if tb == NB - 1:
                            pc.mmdone[go] = tm.n
                        pc.psum_free[pb] = epi(og, tb, psum, tm.n)

        def make_rope_epi(dst_dram, scr, st):
            a_scr, b_scr, qr_scr, tdst = scr

            def epi(og, tb, psum, mm_t):
                sb = (og * NB + tb) % 2

                def f(v, og=og, tb=tb, psum=psum, mm_t=mm_t, sb=sb):
                    v.wait_ge(tm.h, mm_t)
                    if st["scr"][sb]:
                        v.wait_ge(rts[sb].h, st["scr"][sb])
                    ps4 = psum.ap().rearrange("p (h q) -> p h q", h=4)
                    cosv = sap(cosf, tb * DH, [[0, 4], [1, DH]])
                    rotv = bass.AP(psum, 64, [[psum.ap().ap[0][0], 128], [128, 4], [-64, 2], [1, 64]])
                    sinv = bass.AP(sinf, tb * DH, [[sinf.ap().ap[0][0], 128], [0, 4], [64, 2], [1, 64]])
                    a4 = a_scr[sb].ap().rearrange("p (h q) -> p h q", h=4)
                    b4 = b_scr[sb].ap().rearrange("p (h w q) -> p h w q", h=4, w=2)
                    v.tensor_tensor(out=a4, in0=ps4, in1=cosv, op=ALU.mult)
                    v.tensor_tensor(out=b4, in0=rotv, in1=sinv, op=ALU.mult)
                    v.drain()
                    vm.inc(v.tensor_add(qr_scr[sb][:, :], a_scr[sb][:, :], b_scr[sb][:, :]))
                V(f)
                pfree = vm.n
                qr_t = vm.n

                def f(s, sb=sb, qr_t=qr_t):
                    s.wait_ge(vm.h, qr_t)
                    if st["tdst"][sb]:
                        s.wait_ge(sts[sb].h, st["tdst"][sb])
                    rts[sb].inc(s.dma_start_transpose(out=tdst[sb][:, :, :], in_=qr_scr[sb][:, :]), 16)
                SC(f)
                st["scr"][sb] = rts[sb].n
                tr_t = rts[sb].n

                def f(s, og=og, tb=tb, sb=sb, tr_t=tr_t):
                    s.wait_ge(rts[sb].h, tr_t)
                    sts[sb].inc(s.dma_start(
                        out=dst_dram[og * 512:(og + 1) * 512, tb * 128:(tb + 1) * 128]
                        .rearrange("(c p) t -> p c t", p=128),
                        in_=tdst[sb][:, :, :]), 16)
                SY(f)
                st["tdst"][sb] = sts[sb].n
                return pfree
            return epi

        def make_v_epi(dst_dram, scr):
            st = {"scr": {0: 0, 1: 0}}

            def epi(og, tb, psum, mm_t):
                sb = (og * NB + tb) % 2

                def f(v, psum=psum, mm_t=mm_t, sb=sb):
                    v.wait_ge(tm.h, mm_t)
                    if st["scr"][sb]:
                        v.wait_ge(svs[sb].h, st["scr"][sb])
                    vm.inc(v.tensor_copy(out=scr[sb][:, :], in_=psum[:, :]))
                V(f)
                pfree = vm.n
                ve_t = vm.n

                def f(s, og=og, tb=tb, sb=sb, ve_t=ve_t):
                    s.wait_ge(vm.h, ve_t)
                    svs[sb].inc(s.dma_start(out=dst_dram[tb * 128:(tb + 1) * 128,
                                                         og * 512:(og + 1) * 512],
                                            in_=scr[sb][:, :]), 16)
                SY(f)
                st["scr"][sb] = svs[sb].n
                return pfree
            return epi

        def make_res_epi(src, dst, x_scr, xp_scr):
            st = {"x": {0: 0, 1: 0}, "xp": {0: 0, 1: 0}}

            def epi(og, tb, psum, mm_t):
                sb = (og * NB + tb) % 2

                def f(s, og=og, tb=tb, sb=sb):
                    if st["x"][sb]:
                        s.wait_ge(vm.h, st["x"][sb])
                    lxs[sb].inc(s.dma_start(out=x_scr[sb][:, :],
                                            in_=src[tb, :, og * 512:(og + 1) * 512]), 16)
                SY(f)
                x_t = lxs[sb].n

                def f(v, psum=psum, mm_t=mm_t, sb=sb, x_t=x_t):
                    v.wait_ge(tm.h, mm_t)
                    v.wait_ge(lxs[sb].h, x_t)
                    if st["xp"][sb]:
                        v.wait_ge(sxps[sb].h, st["xp"][sb])
                    vm.inc(v.tensor_add(xp_scr[sb][:, :], psum[:, :], x_scr[sb][:, :]))
                V(f)
                pfree = vm.n
                st["x"][sb] = vm.n
                xp_t = vm.n

                def f(s, og=og, tb=tb, sb=sb, xp_t=xp_t):
                    s.wait_ge(vm.h, xp_t)
                    sxps[sb].inc(s.dma_start(out=dst[tb, :, og * 512:(og + 1) * 512],
                                             in_=xp_scr[sb][:, :]), 16)
                SY(f)
                st["xp"][sb] = sxps[sb].n
                return pfree
            return epi

        def attn_proj(jn, is_b, xnT):
            with ExitStack() as pctx:
                pe = pctx.enter_context
                a_scr = [pe(sbuf(f"ra{i}", [128, 512], F32)) for i in range(2)]
                b_scr = [pe(sbuf(f"rb{i}", [128, 512], F32)) for i in range(2)]
                qr_scr = [pe(sbuf(f"rq{i}", [128, 512], BF16)) for i in range(2)]
                tdst = [pe(sbuf(f"rt{i}", [128, 4, 128], BF16,
                                          is_transpose_dest=True)) for i in range(2)]
                v_scr = [pe(sbuf(f"rv{i}", [128, 512], BF16)) for i in range(2)]
                rope_scr = (a_scr, b_scr, qr_scr, tdst)
                rope_st = {"scr": {0: 0, 1: 0}, "tdst": {0: 0, 1: 0}}
                pc = ProjCtx(pe, "wp")

                if not is_b:
                    def wsrc(col0):
                        return lambda og: qkvw[:, col0 + og * 512:col0 + (og + 1) * 512] \
                            .rearrange("(c p) o -> p c o", p=128)

                    def brow(col0):
                        return lambda og: qkvb[:, col0 + og * 512:col0 + (og + 1) * 512]

                    og_loop(pc, xnT, wsrc(E), 8, make_rope_epi(krtA_in, rope_scr, rope_st),
                            bias_src=brow(E))
                    og_loop(pc, xnT, wsrc(2 * E), 8, make_v_epi(vA_in, v_scr),
                            bias_src=brow(2 * E))
                    kin, vin, kout, vout = krtA_in, vA_in, krtA_out, vA_out
                    qsrc, qb, qrt = wsrc(0), brow(0), qrtA
                else:
                    def wsrc_t(wd):
                        return lambda og: wd[:, og * 512:(og + 1) * 512] \
                            .rearrange("(c p) o -> p c o", p=128)

                    og_loop(pc, xnT, wsrc_t(wkT), 2, make_rope_epi(krtB_in, rope_scr, rope_st))
                    og_loop(pc, xnT, wsrc_t(wvT), 2, make_v_epi(vB_in, v_scr))
                    kin, vin, kout, vout = krtB_in, vB_in, krtB_out, vB_out
                    qsrc, qb, qrt = wsrc_t(wqT), None, qrtB

                kv_stg = [(sts[0], sts[0].n), (sts[1], sts[1].n),
                          (svs[0], svs[0].n), (svs[1], svs[1].n)]

                def f(g, kv_stg=kv_stg, kin=kin, vin=vin, kout=kout, vout=vout):
                    for c, n in kv_stg:
                        if n:
                            g.wait_ge(c.h, n)
                    ccs.inc(g.collective_compute("AllGather", ALU.bypass, replica_groups=RG,
                                                 ins=[kin[:, :]], outs=[kout[:, :]]))
                    ccs.inc(g.collective_compute("AllGather", ALU.bypass, replica_groups=RG,
                                                 ins=[vin[:, :]], outs=[vout[:, :]]))
                G(f)
                cc_t = ccs.n

                og_loop(pc, xnT, qsrc, 8, make_rope_epi(qrt, rope_scr, rope_st),
                        bias_src=qb)

            barrier()
            return cc_t

        def attn_rest(jn, is_b, cc_t):
            last = (jn == n_junctions - 1)
            src = x_in if jn == 0 else xres
            dst = y_out if last else xres
            if not is_b:
                kout, vout, qrt = krtA_out, vA_out, qrtA
            else:
                kout, vout, qrt = krtB_out, vB_out, qrtB

            DKV = E if not is_b else KVH * DH
            if True:
                with ExitStack() as pctx:
                    pe = pctx.enter_context
                    ctxe = [pe(sbuf(f"ctxe{i}", [128, 128], BF16)) for i in range(2)]
                    if is_b:
                        p1_s = pe(sbuf("p1_s", [DH, HA, DH], BF16))
                        p2_s = pe(sbuf("p2_s", [DH, HA, DH], BF16))

                        def f(g):
                            p12s.inc(g.dma_start(out=p1_s[:, :, :],
                                                 in_=hp1.rearrange("h d e -> d h e")), 16)
                            p12s.inc(g.dma_start(out=p2_s[:, :, :],
                                                 in_=hp2.rearrange("h d e -> d h e")), 16)
                        G(f)
                        p12_t = p12s.n

                        def f(t, p12_t=p12_t):
                            t.wait_ge(p12s.h, p12_t)
                        T(f)
                    krh = [pe(sbuf(f"krh{i}", [128, 16, 128], BF16)) for i in range(2)]
                    vh = [pe(sbuf(f"vh{i}", [128, 16, 128], BF16)) for i in range(2)]
                    qrh = [pe(sbuf(f"qrh{i}", [128, NB, 128], BF16)) for i in range(2)]
                    probs = pe(sbuf("probs", [128, 2048], BF16))
                    probsT = pe(sbuf("probsT", [128, 16, 128], BF16,
                                               is_transpose_dest=True))
                    mxc = pe(sbuf("mxc", [128, NB], F32))
                    nbc = pe(sbuf("nbc", [128, NB], F32))
                    sec = pe(sbuf("sec", [128, NB], F32))
                    rcc = pe(sbuf("rcc", [128, NB], F32))
                    if is_b:
                        qhh = pe(sbuf("qhh", [128, NB, 128], BF16))
                        khh = pe(sbuf("khh", [128, 16, 128], BF16))
                        hx2 = pe(sbuf("hx2", [128, 2048], BF16))
                        hx2q = pe(sbuf("hx2q", [128, 1024], BF16))
                        hu = pe(sbuf("hu", [128, 512], F32))
                        simh = pe(sbuf("simh", [128, 9216], BF16))
                        gts = pe(sbuf("gts", [128, 2048], BF16))
                        eqs = pe(sbuf("eqs", [128, 2048], BF16))
                        cums = pe(sbuf("cums", [128, 2048], F32))
                        ties = pe(sbuf("ties", [128, 2048], BF16))
                        selu = pe(sbuf("selu", [128, 2048], U8))
                        scm = pe(sbuf("scm", [128, 2048], F32))
                        cjk = pe(sbuf("cjk", [128, 2048], BF16))
                        lo_ = [pe(sbuf(f"lo{i}", [128, NB], F32)) for i in range(2)]
                        hi_ = [pe(sbuf(f"hi{i}", [128, NB], F32)) for i in range(2)]
                        chi = [pe(sbuf(f"chi{i}", [128, NB], F32)) for i in range(2)]
                        mid = pe(sbuf("mid", [128, NB], F32))
                        cnt = pe(sbuf("cnt", [128, NB], F32))
                        cf = pe(sbuf("cf", [128, NB], F32))
                        cf1 = pe(sbuf("cf1", [128, NB], F32))
                        tmp8 = pe(sbuf("tmp8", [128, NB], F32))
                        tmp8b = pe(sbuf("tmp8b", [128, NB], F32))
                        thp2 = pe(sbuf("thp2", [128, NB], F32))
                        mrem = pe(sbuf("mrem", [128, NB], F32))

                    OFF = [256 * (j * (j + 1) // 2) for j in range(NB + 1)]

                    ctxe_free = {0: 0, 1: 0}
                    kv_free = {0: (0, 0), 1: (0, 0)}
                    qr_free = {0: 0, 1: 0}
                    psbig_tm = 0
                    psbig_sm = 0
                    psC_free = 0
                    probs_free = 0
                    probsT_free = 0
                    scm_free = 0
                    hash_free = {0: 0, 1: 0}
                    hh_free = 0

                    for h in range(HA):
                        hb = h % 2
                        kvh = h // 4 if is_b else h
                        kb = (h // 4) % 2 if is_b else h % 2
                        new_kv = (not is_b) or (h % 4 == 0)

                        if new_kv:
                            def f(s, kb=kb, kvh=kvh, g=kv_free[kb]):
                                s.wait_ge(ccs.h, cc_t)
                                if g[0]:
                                    s.wait_ge(tm.h, g[0])
                                if g[1]:
                                    s.wait_ge(vm.h, g[1])
                                for r in range(2):
                                    kvs[kb].inc(s.dma_start(
                                        out=bass.AP(krh[kb], r * 128,
                                                    [[krh[kb].ap().ap[0][0], 128], [256, 8], [1, 128]]),
                                        in_=kout[r * DKV + kvh * 128:r * DKV + (kvh + 1) * 128, :]
                                        .rearrange("p (j q) -> p j q", j=8)), 16)
                                    kvs[kb].inc(s.dma_start(
                                        out=bass.AP(vh[kb], r * 128,
                                                    [[vh[kb].ap().ap[0][0], 128], [256, 8], [1, 128]]),
                                        in_=vout[r * OWN:(r + 1) * OWN, kvh * 128:(kvh + 1) * 128]
                                        .rearrange("(j p) d -> p j d", p=128)), 16)
                            SY(f)
                        kv_t = (kvs[kb], kvs[kb].n)

                        def f(s, h=h, hb=hb, qrt=qrt, g=qr_free[hb]):
                            if g:
                                s.wait_ge(tm.h, g)
                            qs[hb].inc(s.dma_start(
                                out=qrh[hb][:, :, :],
                                in_=qrt[h * 128:(h + 1) * 128, :].rearrange("p (j q) -> p j q", j=8)), 16)
                        SY(f)
                        q_t = (qs[hb], qs[hb].n)

                        krf = krh[kb].ap().rearrange("p j q -> p (j q)")
                        if is_b:
                            khf = khh.ap().rearrange("p j q -> p (j q)")
                            qrf = qrh[hb].ap().rearrange("p j q -> p (j q)")
                            qhf = qhh.ap().rearrange("p j q -> p (j q)")

                            def hash_emit(src_flat, x2buf, nchunks, out_flat, in_t):
                                for cc4 in range(nchunks):
                                    pp = PS[cc4 % 2]

                                    def f(t, cc4=cc4, pp=pp, h=h, in_t=in_t):
                                        t.wait_ge(in_t[0].h, in_t[1])
                                        if hash_free[cc4 % 2]:
                                            t.wait_ge(vm.h, hash_free[cc4 % 2])
                                        tm.inc(t.matmul(pp[:, :], p1_s[:, h, :],
                                                        src_flat[:, cc4 * 512:(cc4 + 1) * 512],
                                                        start=True, stop=True))
                                    T(f)
                                    t1 = tm.n

                                    def f(s, cc4=cc4, pp=pp, h=h, t1=t1):
                                        s.wait_ge(tm.h, t1)
                                        sm.inc(s.activation(out=x2buf[:, cc4 * 512:(cc4 + 1) * 512],
                                                            in_=pp[:, :], func=AF.Silu,
                                                            bias=hb1_s[:, h:h + 1]))
                                    SC(f)
                                    s1t = sm.n

                                    def f(v, cc4=cc4, s1t=s1t):
                                        v.wait_ge(sm.h, s1t)
                                        vm.inc(v.tensor_add(x2buf[:, cc4 * 512:(cc4 + 1) * 512],
                                                            x2buf[:, cc4 * 512:(cc4 + 1) * 512],
                                                            src_flat[:, cc4 * 512:(cc4 + 1) * 512]))
                                    V(f)
                                    x2t = vm.n

                                    def f(t, cc4=cc4, pp=pp, h=h, x2t=x2t):
                                        t.wait_ge(vm.h, x2t)
                                        tm.inc(t.matmul(pp[:, :], p2_s[:, h, :],
                                                        x2buf[:, cc4 * 512:(cc4 + 1) * 512],
                                                        start=True, stop=True))
                                    T(f)
                                    t2_ = tm.n

                                    def f(v, cc4=cc4, pp=pp, h=h, t2_=t2_):
                                        v.wait_ge(tm.h, t2_)
                                        vm.inc(v.scalar_tensor_tensor(
                                            out=hu[:, :], in0=pp[:, :], scalar=hb2_s[:, h:h + 1],
                                            in1=x2buf[:, cc4 * 512:(cc4 + 1) * 512],
                                            op0=ALU.add, op1=ALU.add))
                                    V(f)
                                    hash_free[cc4 % 2] = vm.n
                                    ut = vm.n

                                    def f(s, cc4=cc4, ut=ut):
                                        s.wait_ge(vm.h, ut)
                                        sm.inc(s.activation(out=out_flat[:, cc4 * 512:(cc4 + 1) * 512],
                                                            in_=hu[:, :], func=AF.Sign))
                                    SC(f)

                            hash_emit(krf, hx2, 4, khf, kv_t)
                            hash_emit(qrf, hx2q, 2, qhf, q_t)
                            kh_t = sm.n

                            sim_evac = 0
                            for j in range(NB):
                                kl = 256 * (j + 1)

                                def f(t, j=j, kl=kl, kh_t=kh_t, g=(psbig_tm, psbig_sm), se=sim_evac):
                                    t.wait_ge(sm.h, kh_t)
                                    if j == 0:
                                        if g[0]:
                                            t.wait_ge(vm.h, g[0])
                                        if g[1]:
                                            t.wait_ge(sm.h, g[1])
                                    else:
                                        t.wait_ge(vm.h, se)
                                    nkc = (kl + 511) // 512
                                    for kc in range(nkc):
                                        w = min(512, kl - kc * 512)
                                        t.matmul(ps_big[:, kc * 512:kc * 512 + w],
                                                 qhh[:, j, :], khf[:, kc * 512:kc * 512 + w],
                                                 start=True, stop=(kc != nkc - 1))
                                    wl = kl - (nkc - 1) * 512
                                    tm.inc(t.matmul(ps_big[:, kl - wl:kl], id_s[:, :],
                                                    dmS_s[:, 512 - wl:512], start=False, stop=True))
                                T(f)
                                simt = tm.n

                                def f(v, j=j, kl=kl, simt=simt):
                                    v.wait_ge(tm.h, simt)
                                    vm.inc(v.tensor_copy(out=simh[:, OFF[j]:OFF[j] + kl],
                                                         in_=ps_big[:, :kl]))
                                V(f)
                                sim_evac = vm.n
                            psbig_tm = vm.n
                            psbig_sm = 0

                            def f(v, g=vm.n):
                                v.memset(lo_[0][:, :], -130.0)
                                v.memset(hi_[0][:, :], 382.0)
                                v.memset(chi[0][:, :], 0.0)
                                vm.inc(v.memset(mid[:, :], 126.0))
                            V(f)

                            for it in range(8):
                                pb_ = it % 2
                                nb_ = 1 - pb_
                                gmt_ = gm.n
                                for j in range(NB):
                                    kl = 256 * (j + 1)

                                    def f(v, j=j, kl=kl, it=it, gmt_=gmt_):
                                        if it > 0 and j == 0:
                                            v.wait_ge(gm.h, gmt_)
                                        v.tensor_scalar(out=cjk[:, :kl],
                                                        in0=simh[:, OFF[j]:OFF[j] + kl],
                                                        scalar1=mid[:, j:j + 1], scalar2=0.0,
                                                        op0=ALU.is_ge, op1=ALU.add,
                                                        accum_out=cnt[:, j:j + 1])
                                        vm.inc(v.drain())
                                    V(f)
                                cmp_t = vm.n

                                def f(g, pb_=pb_, nb_=nb_, cmp_t=cmp_t):
                                    g.wait_ge(vm.h, cmp_t)
                                    g.tensor_scalar(out=cf[:, :], in0=cnt[:, :],
                                                    scalar1=float(KREM), scalar2=None, op0=ALU.is_ge)
                                    g.drain()
                                    g.tensor_scalar(out=cf1[:, :], in0=cf[:, :],
                                                    scalar1=-1.0, scalar2=1.0,
                                                    op0=ALU.mult, op1=ALU.add)
                                    g.drain()
                                    for (src_, dst_, sel_) in ((lo_, None, cf), (hi_, None, cf1),
                                                               (chi, cnt, cf1)):
                                        other = mid[:, :] if dst_ is None else dst_[:, :]
                                        g.tensor_sub(tmp8[:, :], other, src_[pb_][:, :])
                                        g.drain()
                                        g.tensor_mul(tmp8b[:, :], tmp8[:, :], sel_[:, :])
                                        g.drain()
                                        g.tensor_add(src_[nb_][:, :], src_[pb_][:, :], tmp8b[:, :])
                                        g.drain()
                                    g.tensor_add(tmp8[:, :], lo_[nb_][:, :], hi_[nb_][:, :])
                                    g.drain()
                                    gm.inc(g.tensor_scalar(out=mid[:, :], in0=tmp8[:, :],
                                                           scalar1=0.5, scalar2=None, op0=ALU.mult))
                                G(f)
                            fb = 0

                            def f(g):
                                g.tensor_scalar(out=mrem[:, :], in0=chi[fb][:, :], scalar1=-1.0,
                                                scalar2=float(KREM), op0=ALU.mult, op1=ALU.add)
                                g.drain()
                                gm.inc(g.tensor_scalar(out=thp2[:, :], in0=lo_[fb][:, :],
                                                       scalar1=2.0, scalar2=None, op0=ALU.add))
                            G(f)
                            bis_t = gm.n
                        else:
                            bis_t = 0
                            kh_t = 0

                        for j in range(NB):
                            kl = 256 * (j + 1)

                            def f(t, j=j, kl=kl, hb=hb, kb=kb, q_t=q_t, kv_t=kv_t,
                                  g=(psbig_tm, psbig_sm), bis_t=bis_t):
                                t.wait_ge(q_t[0].h, q_t[1])
                                t.wait_ge(kv_t[0].h, kv_t[1])
                                if is_b and j == 0:
                                    t.wait_ge(gm.h, bis_t)
                                if j == 0:
                                    if g[0]:
                                        t.wait_ge(vm.h, g[0])
                                    if g[1]:
                                        t.wait_ge(sm.h, g[1])
                                else:
                                    if is_b:
                                        t.wait_ge(vm.h, scores_cons)
                                    else:
                                        t.wait_ge(sm.h, scores_cons)
                                krf_ = krh[kb].ap().rearrange("p j q -> p (j q)")
                                nkc = (kl + 511) // 512
                                last = None
                                for kc in range(nkc):
                                    w = min(512, kl - kc * 512)
                                    last = t.matmul(ps_big[:, kc * 512:kc * 512 + w],
                                                    qrh[hb][:, j, :], krf_[:, kc * 512:kc * 512 + w],
                                                    start=True,
                                                    stop=(is_b or kc != nkc - 1))
                                if not is_b:
                                    wl = kl - (nkc - 1) * 512
                                    last = t.matmul(ps_big[:, kl - wl:kl], id_s[:, :],
                                                    dmA_s[:, 512 - wl:512], start=False, stop=True)
                                tm.inc(last)
                            T(f)
                            sc_t = tm.n
                            if j == NB - 1:
                                qr_free[hb] = tm.n
                                prev = kv_free[kb]
                                kv_free[kb] = (tm.n, prev[1])

                            if is_b:
                                def f(v, j=j, kl=kl, sc_t=sc_t, scm_g=scm_free, bt=bis_t):
                                    v.wait_ge(tm.h, sc_t)
                                    v.wait_ge(gm.h, bt)
                                    if scm_g:
                                        v.wait_ge(sm.h, scm_g)
                                    v.tensor_scalar(out=gts[:, :kl], in0=simh[:, OFF[j]:OFF[j] + kl],
                                                    scalar1=thp2[:, j:j + 1], scalar2=None, op0=ALU.is_ge)
                                    v.tensor_scalar(out=eqs[:, :kl], in0=simh[:, OFF[j]:OFF[j] + kl],
                                                    scalar1=lo_[0][:, j:j + 1], scalar2=None,
                                                    op0=ALU.is_equal)
                                    v.drain()
                                    v.tensor_tensor_scan(out=cums[:, :kl], data0=eqs[:, :kl],
                                                         data1=zerot[:, :kl], initial=0.0,
                                                         op0=ALU.add, op1=ALU.add)
                                    v.drain()
                                    v.scalar_tensor_tensor(out=ties[:, :kl], in0=cums[:, :kl],
                                                           scalar=mrem[:, j:j + 1], in1=eqs[:, :kl],
                                                           op0=ALU.is_le, op1=ALU.mult)
                                    v.drain()
                                    v.tensor_tensor(out=selu[:, :kl], in0=gts[:, :kl],
                                                    in1=ties[:, :kl], op=ALU.max)
                                    v.drain()
                                    v.select(out=scm[:, :kl], mask=selu[:, :kl],
                                             on_true=ps_big[:, :kl], on_false=negt[:, :kl],
                                             add_drain=True)
                                    v.drain()
                                    v.tensor_reduce(mxc[:, j:j + 1], scm[:, :kl], AX, ALU.max,
                                                    negate=True)
                                    v.drain()
                                    vm.inc(v.tensor_scalar(out=nbc[:, j:j + 1], in0=mxc[:, j:j + 1],
                                                           scalar1=float(SCALE), scalar2=None,
                                                           op0=ALU.mult))
                                V(f)
                                scores_cons = vm.n
                                pre_t = vm.n
                                exp_in = scm
                            else:
                                def f(v, j=j, kl=kl, sc_t=sc_t):
                                    v.wait_ge(tm.h, sc_t)
                                    v.tensor_reduce(mxc[:, j:j + 1], ps_big[:, :kl], AX, ALU.max,
                                                    negate=True)
                                    v.drain()
                                    vm.inc(v.tensor_scalar(out=nbc[:, j:j + 1], in0=mxc[:, j:j + 1],
                                                           scalar1=float(SCALE), scalar2=None,
                                                           op0=ALU.mult))
                                V(f)
                                pre_t = vm.n
                                exp_in = ps_big

                            def f(s, j=j, kl=kl, pre_t=pre_t, exp_in=exp_in, pf=probs_free):
                                s.wait_ge(vm.h, pre_t)
                                if pf:
                                    s.wait_ge(ptrs.h, pf)
                                sm.inc(s.activation(out=probs[:, :kl], in_=exp_in[:, :kl],
                                                    func=AF.Exp, bias=nbc[:, j:j + 1],
                                                    scale=float(SCALE), accum_out=sec[:, j:j + 1]))
                            SC(f)
                            exp_t = sm.n
                            if is_b:
                                scm_free = sm.n
                            else:
                                scores_cons = sm.n
                                psbig_sm = sm.n
                                psbig_tm = 0

                            def f(v, j=j, exp_t=exp_t):
                                v.wait_ge(sm.h, exp_t)
                                vm.inc(v.reciprocal(rcc[:, j:j + 1], sec[:, j:j + 1]))
                            V(f)
                            rc_t = vm.n

                            def f(s, j=j, kl=kl, exp_t=exp_t, ptf=probsT_free):
                                s.wait_ge(sm.h, exp_t)
                                if ptf:
                                    s.wait_ge(tm.h, ptf)
                                ptrs.inc(s.dma_start_transpose(out=probsT[:, :kl // 128, :],
                                                               in_=probs[:, :kl]), 16)
                            SC(f)
                            probs_free = ptrs.n
                            pt_t = ptrs.n

                            def f(t, j=j, kl=kl, kb=kb, pt_t=pt_t, pcf=psC_free):
                                t.wait_ge(ptrs.h, pt_t)
                                if pcf:
                                    t.wait_ge(vm.h, pcf)
                                for kc in range(kl // 128):
                                    t.matmul(psC[:, :128], probsT[:, kc, :], vh[kb][:, kc, :],
                                             start=(kc == 0), stop=(kc == kl // 128 - 1))
                                tm.inc(t.drain())
                            T(f)
                            probsT_free = tm.n
                            cx_t = tm.n
                            if j == NB - 1:
                                prev = kv_free[kb]
                                kv_free[kb] = (max(prev[0], tm.n), prev[1])

                            cb = (h * NB + j) % 2

                            def f(v, j=j, h=h, cx_t=cx_t, rc_t=rc_t, cb=cb, cg=ctxe_free[cb]):
                                v.wait_ge(tm.h, cx_t)
                                v.wait_ge(vm.h, rc_t)
                                if cg:
                                    v.wait_ge(ces[cb].h, cg)
                                vm.inc(v.tensor_scalar(out=ctxe[cb][:, :], in0=psC[:, :128],
                                                       scalar1=rcc[:, j:j + 1], scalar2=None,
                                                       op0=ALU.mult))
                            V(f)
                            psC_free = vm.n
                            ce_t = vm.n

                            def f(s, j=j, h=h, cb=cb, ce_t=ce_t):
                                s.wait_ge(vm.h, ce_t)
                                ces[cb].inc(s.dma_start(
                                    out=ctx_hbm[j * 128:(j + 1) * 128, h * 128:(h + 1) * 128],
                                    in_=ctxe[cb][:, :]), 16)
                            SY(f)
                            ctxe_free[cb] = ces[cb].n
                        if is_b:
                            psbig_tm = scores_cons
                            psbig_sm = 0

                barrier()

            with ExitStack() as pctx:
                pe = pctx.enter_context
                ctxT = pe(sbuf(f"ctxT{jn}", [128, NCH, OWN], BF16))
                ctxl = [pe(sbuf(f"ctxl{i}", [128, E], BF16)) for i in range(2)]
                cl_done = {}
                for j in range(NB):
                    cb = j % 2

                    def f(s, j=j, cb=cb):
                        if j >= 2:
                            s.wait_ge(cts[cb].h, cl_done[j - 2])
                        cls[cb].inc(s.dma_start(out=ctxl[cb][:, :],
                                                in_=ctx_hbm[j * 128:(j + 1) * 128, :]), 16)
                    SY(f)
                    cl_t = cls[cb].n

                    def f(s, j=j, cb=cb, cl_t=cl_t):
                        s.wait_ge(cls[cb].h, cl_t)
                        cts[cb].inc(s.dma_start_transpose(
                            out=ctxT[:, :, j * 128:(j + 1) * 128], in_=ctxl[cb][:, :]), 16)
                    SC(f)
                    cl_done[j] = cts[cb].n
                barrier()

                x_scr = [pe(sbuf(f"ox{i}", [128, 512], F32)) for i in range(2)]
                xp_scr = [pe(sbuf(f"oxp{i}", [128, 512], F32)) for i in range(2)]
                wd = outw if not is_b else woT

                og_loop(ProjCtx(pe, "wo"), ctxT,
                        lambda og: wd[:, og * 512:(og + 1) * 512].rearrange("(c p) o -> p c o", p=128),
                        8, make_res_epi(src, dst, x_scr, xp_scr),
                        bias_src=(None if is_b else
                                  (lambda og: outb[:, og * 512:(og + 1) * 512])))
            barrier()

        def ffn_junction(jn, xnT):
            last = (jn == n_junctions - 1)
            src = x_in if jn == 0 else xres
            dst = y_out if last else xres

            with ExitStack() as pctx:
                pe = pctx.enter_context
                wr_s = pe(sbuf(f"wr_s{jn}", [128, NCH, 8], BF16))
                la_s = pe(sbuf(f"la_s{jn}", [128, NCH, 128], BF16))
                lb_s = pe(sbuf(f"lb_s{jn}", [128, E], BF16))

                def f(g):
                    fws.inc(g.dma_start(out=wr_s[:, :, :],
                                        in_=wrT.rearrange("(c p) e -> p c e", p=128)), 16)
                    fws.inc(g.dma_start(out=la_s[:, :, :],
                                        in_=laT.rearrange("(c p) e -> p c e", p=128)), 16)
                    fws.inc(g.dma_start(out=lb_s[:, :], in_=lbT[:, :]), 16)
                G(f)
                fw_t = fws.n

                def f(t, fw_t=fw_t):
                    t.wait_ge(fws.h, fw_t)
                T(f)
                hwT = pe(sbuf(f"hwT{jn}", [128, NB, 128], BF16, is_transpose_dest=True))
                logc = {k: pe(sbuf(f"fl_{k}{jn}", [128, 1], F32)) for k in ["lmx", "m1", "m2", "den", "rden"]}
                lex = pe(sbuf(f"lex{jn}", [128, 8], F32))
                le2 = pe(sbuf(f"le2{jn}", [128, 8], F32))
                e1 = pe(sbuf(f"e1{jn}", [128, 8], U8))
                sel2 = pe(sbuf(f"sel2{jn}", [128, 8], F32))
                wexp = pe(sbuf(f"wexp{jn}", [128, 8], F32))
                hw = pe(sbuf(f"hwv{jn}", [128, 128], BF16))

                rtr_free = 0
                hw_tr = 0
                for tb in range(NB):
                    def f(t, tb=tb, g=rtr_free):
                        if g:
                            t.wait_ge(vm.h, g)
                        for c in range(NCH):
                            t.matmul(psC[:, :8], xnT[:, c, tb * 128:(tb + 1) * 128],
                                     wr_s[:, c, :], start=(c == 0), stop=(c == NCH - 1))
                        for c in range(NCH):
                            t.matmul(psD[:, :128], xnT[:, c, tb * 128:(tb + 1) * 128],
                                     la_s[:, c, :], start=(c == 0), stop=(c == NCH - 1))
                        tm.inc(t.drain())
                    T(f)
                    rt_t = tm.n

                    def f(v, rt_t=rt_t):
                        v.wait_ge(tm.h, rt_t)
                        vm.inc(v.tensor_reduce(logc["lmx"][:, :], psC[:, :8], AX, ALU.max,
                                               negate=True))
                    V(f)
                    lm_t = vm.n

                    def f(s, lm_t=lm_t):
                        s.wait_ge(vm.h, lm_t)
                        sm.inc(s.activation(out=lex[:, :], in_=psC[:, :8], func=AF.Exp,
                                            bias=logc["lmx"][:, :], scale=1.0))
                    SC(f)
                    le_t = sm.n

                    def f(v, le_t=le_t, hwg=hw_tr):
                        v.wait_ge(sm.h, le_t)
                        if hwg:
                            v.wait_ge(htrs.h, hwg)
                        v.tensor_reduce(logc["m1"][:, :], lex[:, :], AX, ALU.max)
                        v.drain()
                        v.tensor_scalar(out=e1[:, :], in0=lex[:, :], scalar1=logc["m1"][:, :],
                                        scalar2=None, op0=ALU.is_ge)
                        v.drain()
                        v.select(out=le2[:, :], mask=e1[:, :], on_true=negt[:, :8],
                                 on_false=lex[:, :], add_drain=True)
                        v.drain()
                        v.tensor_reduce(logc["m2"][:, :], le2[:, :], AX, ALU.max)
                        v.drain()
                        v.tensor_scalar(out=sel2[:, :], in0=lex[:, :], scalar1=logc["m2"][:, :],
                                        scalar2=None, op0=ALU.is_ge)
                        v.drain()
                        v.tensor_add(logc["den"][:, :], logc["m1"][:, :], logc["m2"][:, :])
                        v.drain()
                        v.reciprocal(logc["rden"][:, :], logc["den"][:, :])
                        v.drain()
                        v.tensor_mul(wexp[:, :], lex[:, :], sel2[:, :])
                        v.drain()
                        v.tensor_scalar(out=wexp[:, :], in0=wexp[:, :], scalar1=logc["rden"][:, :],
                                        scalar2=2.0, op0=ALU.mult, op1=ALU.mult)
                        v.drain()
                        for e8 in range(8):
                            v.tensor_scalar(out=hw[:, e8 * 16:(e8 + 1) * 16],
                                            in0=psD[:, e8 * 16:(e8 + 1) * 16],
                                            scalar1=wexp[:, e8:e8 + 1], scalar2=None, op0=ALU.mult)
                        vm.inc(v.drain())
                    V(f)
                    rtr_free = vm.n
                    hw_t = vm.n

                    def f(s, tb=tb, hw_t=hw_t, g=hw_tr):
                        s.wait_ge(vm.h, hw_t)
                        htrs.inc(s.dma_start_transpose(out=hwT[:, tb:tb + 1, :], in_=hw[:, :]), 16)
                    SC(f)
                    hw_tr = htrs.n
                barrier()

                x_scr = [pe(sbuf(f"fx{i}", [128, 512], F32)) for i in range(2)]
                xp_scr = [pe(sbuf(f"fxp{i}", [128, 512], F32)) for i in range(2)]

                og_loop(ProjCtx(pe, "wb"), xnT,
                        lambda og: wbT[:, og * 512:(og + 1) * 512].rearrange("(c p) o -> p c o", p=128),
                        8, make_res_epi(src, dst, x_scr, xp_scr),
                        bias_src=lambda og: bbase[:, og * 512:(og + 1) * 512],
                        lora=True, hwT=hwT, lb_s=lb_s)
            barrier()

        for jn in range(n_junctions):
            src_d = x_in if jn == 0 else xres
            is_attn = jn in (0, 2)
            with ExitStack() as js:
                xnT = js.enter_context(sbuf(f"xnT{jn}", [128, NCH, OWN], BF16))
                normalize_phase(src_d, xnT)
                if is_attn:
                    cc_t = attn_proj(jn, is_b=(jn == 2), xnT=xnT)
                else:
                    ffn_junction(jn, xnT)
            if is_attn:
                attn_rest(jn, is_b=(jn == 2), cc_t=cc_t)

        def f(s):
            for c in ALL_DMA:
                if c.n:
                    s.wait_ge(c.h, c.n)
        SY(f)
        barrier()

    return nc


def kernel(**inputs):
    g = {k: np.asarray(v, dtype=np.float32) for k, v in inputs.items()}
    x = g["L0_attn_a_hidden_states"]

    shared = {
        "qkvw": np.ascontiguousarray(g["L0_attn_a_qkv_w"]),
        "qkvb": np.ascontiguousarray(g["L0_attn_a_qkv_b"].reshape(1, -1)),
        "outw": np.ascontiguousarray(g["L0_attn_a_out_w"]),
        "outb": np.ascontiguousarray(g["L0_attn_a_out_b"].reshape(1, -1)),
        "wqT": np.ascontiguousarray(g["L0_attn_b_Wq"].T),
        "wkT": np.ascontiguousarray(g["L0_attn_b_Wk"].T),
        "wvT": np.ascontiguousarray(g["L0_attn_b_Wv"].T),
        "woT": np.ascontiguousarray(g["L0_attn_b_Wo"].T),
        "hp1": np.ascontiguousarray(g["L0_attn_b_hash_proj1"]),
        "hp2": np.ascontiguousarray(g["L0_attn_b_hash_proj2"]),
        "hb1": np.ascontiguousarray(g["L0_attn_b_hash_bias1"].reshape(HA, DH).T),
        "hb2": np.ascontiguousarray(g["L0_attn_b_hash_bias2"].reshape(HA, DH).T),
        "wbT": np.ascontiguousarray(g["L0_ffn_W_base"].T),
        "bbase": np.ascontiguousarray(g["L0_ffn_b_base"].reshape(1, -1)),
        "wrT": np.ascontiguousarray(g["L0_ffn_W_router"].T),
        "laT": np.ascontiguousarray(g["L0_ffn_lora_A"].transpose(2, 0, 1).reshape(E, 128)),
        "lbT": np.ascontiguousarray(g["L0_ffn_lora_B"].transpose(0, 2, 1).reshape(128, E)),
        "id128": np.eye(128, dtype=np.float32),
    }

    tri_mask = np.triu(np.full((TPB, TPB), 1.0, np.float32), k=1)
    in_maps = []
    for c in range(8):
        b, i = c // 2, c % 2
        xb = np.ascontiguousarray(x[b].reshape(16, TPB, E)[i::2])
        pos = (np.arange(16, dtype=np.float32).reshape(16, 1)[i::2] * TPB
               + np.arange(TPB, dtype=np.float32).reshape(1, TPB)).T
        z = np.zeros((TPB, 256), np.float32)
        if i == 0:
            dmA_ = np.concatenate([z, tri_mask * NEGM, np.full((TPB, TPB), NEGM, np.float32)], axis=1)
            dmS_ = np.concatenate([z, tri_mask * SIMM, np.full((TPB, TPB), SIMM, np.float32)], axis=1)
        else:
            dmA_ = np.concatenate([z, np.zeros((TPB, TPB), np.float32), tri_mask * NEGM], axis=1)
            dmS_ = np.concatenate([z, np.zeros((TPB, TPB), np.float32), tri_mask * SIMM], axis=1)
        m = dict(shared)
        m["x"] = xb
        m["posv"] = np.ascontiguousarray(pos)
        m["dmA"] = np.ascontiguousarray(dmA_)
        m["dmS"] = np.ascontiguousarray(dmS_)
        in_maps.append(m)

    nc = build_nc(N_JUNCTIONS)
    res = run_bass_kernel_spmd(nc, in_maps, core_ids=list(range(8)))

    out = np.empty((B, S, E), dtype=np.float32)
    for c in range(8):
        b, i = c // 2, c % 2
        out[b].reshape(16, TPB, E)[i::2] = res.results[c]["y"].reshape(NB, TPB, E)
    return out
